# version 05 (final)
# baseline (speedup 1.0000x reference)
from contextlib import ExitStack

import numpy as np
import concourse.bass as bass
import concourse.mybir as mybir
from concourse.bass_utils import run_bass_kernel_spmd
from concourse.bass_core_rust import IndirectOffsetOnAxis

I32 = mybir.dt.int32

F32 = mybir.dt.float32
BF16 = mybir.dt.bfloat16
AF = mybir.ActivationFunctionType
ALU = mybir.AluOpType
AX = mybir.AxisListType

ENGS = ("sync", "tensor", "vector", "scalar", "gpsimd")

_NAMECTR = [0]


def _uname(name):
    _NAMECTR[0] += 1
    return f"{name}_{_NAMECTR[0]}"

NCORES = 8
BPC = 2
S = 2048
D = 1024
H = 16
HD = 64
NT = S // 128
DT = D // 128
KJ = 21
EPS = 1e-5
SOFT_OFF = 40.0
DEBUG_L0 = False


class Sem:
    __slots__ = ("h", "n")

    def __init__(self, h):
        self.h = h
        self.n = 0


class Prog:

    CHAINED = ("vector", "scalar", "gpsimd")

    def __init__(self, nc, ctx):
        self.nc = nc
        self.ctx = ctx
        self.q = {e: [] for e in ENGS}
        self._semno = 0
        self.deferred = []
        self.chain = {}
        for e in self.CHAINED:
            self.chain[e] = self.sem(f"chain_{e}")

    def sem(self, name=None):
        self._semno += 1
        nm = f"{name or 's'}_{self._semno}"
        h = self.ctx.enter_context(self.nc.semaphore(nm))
        return Sem(h)

    def wait(self, eng, sem, val=None):
        v = sem.n if val is None else val
        if v > 0:
            self.q[eng].append(lambda e, h=sem.h, v=v: e.wait_ge(h, v))

    def emit(self, eng, fn, inc=None, inc_n=1):
        ch = self.chain.get(eng)
        if ch is not None:
            if ch.n > 0:
                self.q[eng].append(lambda e, h=ch.h, v=ch.n: e.wait_ge(h, v))
            if inc is None or inc is ch:
                ch.n += 1

                def run(e, fn=fn, ch_h=ch.h):
                    fn(e).then_inc(ch_h, 1)
                self.q[eng].append(run)
            else:
                inc.n += inc_n

                def run(e, fn=fn, h=inc.h, k=inc_n):
                    fn(e).then_inc(h, k)
                self.q[eng].append(run)
            return
        if inc is not None:
            inc.n += inc_n

            def run(e, fn=fn, h=inc.h, k=inc_n):
                fn(e).then_inc(h, k)
            self.q[eng].append(run)
        else:
            self.q[eng].append(fn)

    def dma(self, eng, out, in_, inc=None, **kw):
        if inc is not None:
            inc.n += 16

            def run(e, out=out, in_=in_, h=inc.h, kw=kw):
                e.dma_start(out=out, in_=in_, **kw).then_inc(h, 16)
            self.q[eng].append(run)
        else:
            self.q[eng].append(
                lambda e, out=out, in_=in_, kw=kw: e.dma_start(out=out, in_=in_, **kw))

    def barrier(self, tag=""):
        s = self.sem("bar" + tag)
        for e in ENGS:
            self.emit(e, lambda en: en.nop(), inc=s, inc_n=1)
        for e in ENGS:
            self.wait(e, s, len(ENGS))
        return s

    def realize(self):
        with self.nc.Block() as block:
            for name in ENGS:
                if not self.q[name]:
                    continue

                def body(eng, name=name):
                    for fn in self.q[name]:
                        fn(eng)
                getattr(block, name)(body)


def ln_emit(p, x_ap, out_ap, sqs, st, sA, sV, rows=128, cols=D):
    p.emit("vector", lambda e: e.tensor_tensor(out=sqs[0:rows, 0:cols], in_=None, in0=x_ap,
                                               in1=x_ap, op=ALU.mult)
           if False else e.tensor_tensor(out=sqs[0:rows, 0:cols], in0=x_ap, in1=x_ap,
                                         op=ALU.mult))
    p.emit("vector", lambda e: e.reduce_sum(out=st[0:rows, 1:2], in_=sqs[0:rows, 0:cols],
                                            axis=AX.X), inc=sV)
    p.emit("vector", lambda e: e.reduce_sum(out=st[0:rows, 0:1], in_=x_ap, axis=AX.X), inc=sV)
    p.wait("vector", sV)
    p.emit("vector", lambda e: e.tensor_scalar(out=st[0:rows, 2:3], in0=st[0:rows, 0:1],
                                               scalar1=1.0 / cols, scalar2=None,
                                               op0=ALU.mult), inc=sV)
    p.wait("vector", sV)
    p.emit("vector", lambda e: e.tensor_tensor(out=st[0:rows, 3:4], in0=st[0:rows, 2:3],
                                               in1=st[0:rows, 2:3], op=ALU.mult), inc=sV)
    p.wait("vector", sV)
    p.emit("vector", lambda e: e.scalar_tensor_tensor(out=st[0:rows, 4:5], in0=st[0:rows, 1:2],
                                                      scalar=1.0 / cols, in1=st[0:rows, 3:4],
                                                      op0=ALU.mult, op1=ALU.subtract), inc=sV)
    p.wait("scalar", sV)
    p.emit("scalar", lambda e: e.activation(out=st[0:rows, 5:6], in_=st[0:rows, 4:5],
                                            func=AF.Ln, bias=st[0:rows, 7:8], scale=1.0), inc=sA)
    p.wait("scalar", sA)
    p.emit("scalar", lambda e: e.activation(out=st[0:rows, 6:7], in_=st[0:rows, 5:6],
                                            func=AF.Exp, scale=-0.5), inc=sA)
    p.wait("vector", sA)
    p.emit("vector", lambda e: e.tensor_tensor(out=st[0:rows, 3:4], in0=st[0:rows, 2:3],
                                               in1=st[0:rows, 6:7], op=ALU.mult), inc=sV)
    p.wait("vector", sV)
    p.emit("vector", lambda e: e.scalar_tensor_tensor(
        out=out_ap, in0=x_ap, scalar=st[0:rows, 6:7],
        in1=st[0:rows, 3:4].broadcast_to([rows, cols]),
        op0=ALU.mult, op1=ALU.subtract), inc=sV)


def build_l0attn(nc, p=None, xout_ap=None, shared=None):
    own = p is None
    ctx = nc.ctx
    x_in = nc.dram_tensor("x", [BPC, S, D], F32, kind="ExternalInput").ap()
    w_in = nc.dram_tensor("w_in", [D, 3 * D], F32, kind="ExternalInput").ap()
    w_gate = nc.dram_tensor("w_gate", [D, 2 * D], F32, kind="ExternalInput").ap()
    b_gate = nc.dram_tensor("b_gate", [1, 2 * D], F32, kind="ExternalInput").ap()
    amp = nc.dram_tensor("amp", [H, KJ], F32, kind="ExternalInput").ap()
    sharp = nc.dram_tensor("sharp", [H, KJ], F32, kind="ExternalInput").ap()
    offs = nc.dram_tensor("offs", [H, KJ], F32, kind="ExternalInput").ap()
    x_out = xout_ap if xout_ap is not None else \
        nc.dram_tensor("x1", [BPC, S, D], F32, kind="ExternalOutput").ap()
    if DEBUG_L0:
        dbg = {
            "d_ln1T0": nc.dram_tensor("d_ln1T0", [128, S], BF16, kind="ExternalOutput").ap(),
            "d_KT0": nc.dram_tensor("d_KT0", [128, S], BF16, kind="ExternalOutput").ap(),
            "d_QT0": nc.dram_tensor("d_QT0", [128, S], BF16, kind="ExternalOutput").ap(),
            "d_V0": nc.dram_tensor("d_V0", [128, H * (HD + 1)], BF16, kind="ExternalOutput").ap(),
            "d_QX": nc.dram_tensor("d_QX", [HD + 1, S], BF16, kind="ExternalOutput").ap(),
            "d_qn": nc.dram_tensor("d_qn", [1, S], BF16, kind="ExternalOutput").ap(),
            "d_bias": nc.dram_tensor("d_bias", [128, S], F32, kind="ExternalOutput").ap(),
            "d_scarg": nc.dram_tensor("d_scarg", [128, S], F32, kind="ExternalOutput").ap(),
            "d_wt": nc.dram_tensor("d_wt", [128, S], BF16, kind="ExternalOutput").ap(),
            "d_pv": nc.dram_tensor("d_pv", [HD + 1, S], F32, kind="ExternalOutput").ap(),
            "d_attnT0": nc.dram_tensor("d_attnT0", [128, S], BF16, kind="ExternalOutput").ap(),
            "d_tzrev": nc.dram_tensor("d_tzrev", [H, 2 * S - 1], F32, kind="ExternalOutput").ap(),
            "d_hrow": nc.dram_tensor("d_hrow", [1, 64], F32, kind="ExternalOutput").ap(),
            "d_ln1b": nc.dram_tensor("d_ln1b", [128, D], BF16, kind="ExternalOutput").ap(),
            "d_xt": nc.dram_tensor("d_xt", [128, D], F32, kind="ExternalOutput").ap(),
            "d_qxrows": nc.dram_tensor("d_qxrows", [H, S], BF16, kind="ExternalOutput").ap(),
            "d_hrows": nc.dram_tensor("d_hrows", [H, 64], F32, kind="ExternalOutput").ap(),
            "d_rsum": nc.dram_tensor("d_rsum", [H, S], F32, kind="ExternalOutput").ap(),
            "d_nsq": nc.dram_tensor("d_nsq", [64, S], BF16, kind="ExternalOutput").ap(),
            "d_ksums": nc.dram_tensor("d_ksums", [1, S], F32, kind="ExternalOutput").ap(),
        }
    tzrev = nc.dram_tensor("tzrev_scr", [H, 2 * S - 1], F32, kind="Internal")
    srow = nc.dram_tensor("srow_scr", [1, H], F32, kind="Internal")

    top = ExitStack()
    sb = lambda name, shape, dt: top.enter_context(nc.sbuf_tensor(_uname(name), shape, dt))

    KT = [sb(f"KT{i}", [128, S], BF16) for i in range(DT)]
    QT = [sb(f"QT{i}", [128, S], BF16) for i in range(DT)]
    Vst = [sb(f"Vst{i}", [128, H * (HD + 1)], BF16) for i in range(NT)]
    TZC = 512
    tzs = sb("tzs", [H, TZC], F32)
    tzd = sb("tzd", [H, TZC], F32)
    tz2 = sb("tz2", [H, TZC], F32)
    maxparts = sb("maxparts", [H, 8], F32)
    prm = sb("prm", [H, 4 * KJ], F32)
    maxTs = sb("maxTs", [H, 1], F32)
    hrow = sb("hrow", [1, 64], F32)
    idxt = sb("idxt", [128, 1], I32)
    if shared is None:
        idxbase = sb("idxbase", [128, 1], I32)
        ones_col = sb("ones_col", [128, 1], BF16)
        ones_row = sb("ones_row", [1, 128], BF16)
        ones_row_f = sb("ones_row_f", [1, 128], F32)
        ident = sb("ident", [128, 128], BF16)
    else:
        idxbase, ones_col, ones_row, ones_row_f, ident = (
            shared["idxbase"], shared["ones_col"], shared["ones_row"],
            shared["ones_row_f"], shared["ident"])

    if own:
        p = Prog(nc, ctx)
    sD = p.sem("sD")
    sPE = p.sem("sPE")
    sV = p.chain["vector"]
    sA = p.chain["scalar"]
    sG = p.chain["gpsimd"]
    sB = p.sem("sB")
    sO = p.sem("sO")

    if shared is None:
        p.emit("gpsimd", lambda e: e.memset(ident[:], 1.0))
        p.emit("gpsimd", lambda e: e.affine_select(out=ident[:], in_=ident[:],
                                                   pattern=[[1, 128]],
                                                   compare_op=ALU.is_equal, fill=0.0, base=0,
                                                   channel_multiplier=-1))
        p.emit("gpsimd", lambda e: e.iota(idxbase[:], [[0, 1]], base=0, channel_multiplier=-1))
        p.emit("gpsimd", lambda e: e.memset(ones_col[:], 1.0))
        p.emit("gpsimd", lambda e: e.memset(ones_row[:], 1.0))
        p.emit("gpsimd", lambda e: e.memset(ones_row_f[:], 1.0), inc=sG)
    for i in range(NT):
        p.emit("gpsimd", lambda e, i=i: e.memset(
            bass.AP(Vst[i], HD, [[H * (HD + 1), 128], [HD + 1, H]]), 1.0))
    p.emit("vector", lambda e: e.memset(hrow[0:1, 0:64], 0.0))
    consts_g = sG.n
    p.dma("sync", prm[0:H, 0 * KJ:1 * KJ], amp[:, :], inc=sD)
    p.dma("sync", prm[0:H, 1 * KJ:2 * KJ], sharp[:, :], inc=sD)
    p.dma("sync", prm[0:H, 2 * KJ:3 * KJ], offs[:, :], inc=sD)
    p.wait("scalar", sD)
    p.emit("scalar", lambda e: e.activation(out=prm[0:H, 3 * KJ:4 * KJ],
                                            in_=prm[0:H, 1 * KJ:2 * KJ],
                                            func=AF.Abs), inc=sA)
    p.wait("vector", sA)
    p.emit("vector", lambda e: e.tensor_scalar(out=prm[0:H, 3 * KJ:4 * KJ],
                                               in0=prm[0:H, 3 * KJ:4 * KJ],
                                               scalar1=-1.0, scalar2=None, op0=ALU.mult), inc=sV)
    NW = 2 * S - 1
    sTz = p.sem("sTz")
    for c in range(0, NW, TZC):
        w = min(TZC, NW - c)
        p.wait("gpsimd", sV)
        p.emit("gpsimd", lambda e, c=c, w=w: e.iota(
            tzd[0:H, 0:w], [[-1, w]], base=S - 1 - c, channel_multiplier=0,
            allow_small_or_imprecise_dtypes=True), inc=sG)
        p.wait("vector", sG)
        p.wait("vector", sD)
        p.emit("vector", lambda e, w=w: e.memset(tzs[0:H, 0:w], 0.0))
        for j in range(KJ):
            p.emit("vector", lambda e, j=j, w=w: e.tensor_scalar(
                out=tz2[0:H, 0:w], in0=tzd[0:H, 0:w],
                scalar1=prm[0:H, 2 * KJ + j:2 * KJ + j + 1],
                scalar2=None, op0=ALU.subtract), inc=sV)
            p.wait("scalar", sV)
            p.emit("scalar", lambda e, w=w: e.activation(out=tz2[0:H, 0:w], in_=tz2[0:H, 0:w],
                                                         func=AF.Square))
            p.emit("scalar", lambda e, j=j, w=w: e.activation(
                out=tz2[0:H, 0:w], in_=tz2[0:H, 0:w], func=AF.Exp,
                scale=prm[0:H, 3 * KJ + j:3 * KJ + j + 1]), inc=sA)
            p.wait("vector", sA)
            p.emit("vector", lambda e, j=j, w=w: e.scalar_tensor_tensor(
                out=tzs[0:H, 0:w], in0=tz2[0:H, 0:w], scalar=prm[0:H, j:j + 1],
                in1=tzs[0:H, 0:w], op0=ALU.mult, op1=ALU.add), inc=sV)
        p.emit("vector", lambda e, c=c, w=w: e.reduce_max(
            out=maxparts[0:H, c // TZC:c // TZC + 1], in_=tzs[0:H, 0:w], axis=AX.X), inc=sV)
        p.wait("vector", sV)
        p.wait("sync", sV)
        p.dma("sync", bass.AP(tzrev, c, [[NW, H], [1, w]]), tzs[0:H, 0:w],
              inc=(sTz if c + TZC >= NW else sD))
    p.emit("vector", lambda e: e.reduce_max(out=maxTs[0:H, :], in_=maxparts[0:H, :], axis=AX.X),
           inc=sV)
    p.wait("sync", sV)
    sScr = p.sem("sScr")
    p.dma("sync", srow.ap(), maxTs[0:H, :], inc=sScr)
    p.wait("sync", sScr)
    p.dma("sync", hrow[0:1, 0:H], srow.ap(), inc=sScr)
    tz_ready = (sTz, sScr)

    bgb_done = [False]

    def _phase_ab(b, ph):
        psb = lambda name, shape, dt: ph.enter_context(nc.sbuf_tensor(_uname(name), shape, dt))
        pps = lambda name, shape, dt: ph.enter_context(nc.psum_tensor(_uname(name), shape, dt))
        if True:
            xt = psb("xt", [128, D], F32)
            sqs = psb("sqs", [128, D], F32)
            st = psb("st", [128, 8], F32)
            ln1b = psb("ln1b", [128, D], BF16)
            ln1T = [psb(f"ln1T{i}", [128, S], BF16) for i in range(DT)]
            wib = [psb(f"wib{i}", [128, 3 * D], BF16) for i in range(DT)]
            wldf = psb("wldf", [128, 768], F32)
            ps_tr = pps("ps_tr", [128, D], BF16)
            ps_p = pps("ps_p", [128, S], F32)
            ps_v = pps("ps_v", [128, 512], F32)

            p.emit("vector", lambda e: e.memset(st[:, 7:8], EPS), inc=sV)
            for kt in range(DT):
                for cc in range(4):
                    p.wait("sync", sA)
                    p.dma("sync", wldf[:, :], w_in[kt * 128:(kt + 1) * 128, cc * 768:(cc + 1) * 768],
                          inc=sD)
                    p.wait("scalar", sD)
                    p.emit("scalar", lambda e, kt=kt, cc=cc: e.activation(
                        out=wib[kt][:, cc * 768:(cc + 1) * 768], in_=wldf[:, :],
                        func=AF.Copy), inc=sA)
            wib_ready = sA.n

            for nt in range(NT):
                p.wait("sync", sV)
                p.wait("sync", sA)
                p.dma("sync", xt[:, :], x_in[b, nt * 128:(nt + 1) * 128, :], inc=sD)
                p.wait("vector", sD)
                p.wait("scalar", sD)
                p.wait("vector", sPE)
                ln_emit(p, xt[:, :], ln1b[:, :], sqs, st, sA, sV)
                p.wait("tensor", sV)
                p.wait("tensor", sA, wib_ready)
                p.wait("tensor", sG, consts_g)
                for j in range(DT):
                    p.emit("tensor", lambda e, j=j: e.transpose(
                        ps_tr[:, j * 128:(j + 1) * 128],
                        ln1b[:, j * 128:(j + 1) * 128], ident[:]),
                        inc=sPE if j == DT - 1 else None)
                p.wait("scalar", sPE)
                for j in range(DT):
                    p.emit("scalar", lambda e, j=j, nt=nt: e.activation(
                        out=ln1T[j][:, nt * 128:(nt + 1) * 128],
                        in_=ps_tr[:, j * 128:(j + 1) * 128], func=AF.Copy),
                        inc=sA if j == DT - 1 else None)
                p.wait("tensor", sA)
            lnT_ready = sA.n

            p.wait("tensor", sA, lnT_ready)
            for ot in list(range(0, DT)) + list(range(2 * DT, 3 * DT)):
                dst = KT[ot] if ot < DT else QT[ot - 2 * DT]
                for c4 in range(4):
                    for kt in range(DT):
                        p.emit("tensor", lambda e, ot=ot, c4=c4, kt=kt: e.matmul(
                            ps_p[:, c4 * 512:(c4 + 1) * 512],
                            wib[kt][:, ot * 128:(ot + 1) * 128],
                            ln1T[kt][:, c4 * 512:(c4 + 1) * 512],
                            start=(kt == 0), stop=(kt == DT - 1)),
                            inc=sPE if (kt == DT - 1 and c4 == 3) else None)
                p.wait("scalar", sPE)
                p.emit("scalar", lambda e, dst=dst: e.activation(out=dst[:, :], in_=ps_p[:, :],
                                                                 func=AF.Copy), inc=sA)
                p.wait("tensor", sA)
            for nt in range(NT):
                for oc in range(2):
                    for kt in range(DT):
                        p.emit("tensor", lambda e, nt=nt, oc=oc, kt=kt: e.matmul(
                            ps_v[:, :],
                            ln1T[kt][:, nt * 128:(nt + 1) * 128],
                            wib[kt][:, D + oc * 512:D + (oc + 1) * 512],
                            start=(kt == 0), stop=(kt == DT - 1)),
                            inc=sPE if kt == DT - 1 else None)
                    p.wait("scalar", sPE)
                    for hh in range(8):
                        h = oc * 8 + hh
                        p.emit("scalar", lambda e, nt=nt, hh=hh, h=h: e.activation(
                            out=Vst[nt][:, h * (HD + 1):h * (HD + 1) + HD],
                            in_=ps_v[:, hh * HD:(hh + 1) * HD], func=AF.Copy),
                            inc=sA if hh == 7 else None)
                    p.wait("tensor", sA)
            if DEBUG_L0:
                p.wait("sync", sA)
                p.wait("sync", sV)
                p.dma("sync", dbg["d_ln1T0"], ln1T[0][:, :], inc=sO)
                p.dma("sync", dbg["d_ln1b"], ln1b[:, :], inc=sO)
                p.dma("sync", dbg["d_xt"], xt[:, :], inc=sO)
                p.wait("sync", sO)
            p.wait("sync", sO)
            p.barrier("ab%d" % b)

    def _phase_c(b, ph, attnT):
        psb = lambda name, shape, dt: ph.enter_context(nc.sbuf_tensor(_uname(name), shape, dt))
        pps = lambda name, shape, dt: ph.enter_context(nc.psum_tensor(_uname(name), shape, dt))
        if True:
            if True:
                KX = psb("KX", [HD + 1, S], BF16)
                QX = psb("QX", [HD + 1, S], BF16)
                nsq = psb("nsq", [64, S], BF16)
                qn = psb("qn", [1, S], BF16)
                rrec = psb("rrec", [1, S], F32)
                rb = psb("rb", [64, S], F32)
                bias0 = psb("bias0", [128, S], F32)
                scArg = psb("scArg", [128, S], F32)
                WT = [psb(f"WTb{i}", [128, S], BF16) for i in range(2)]
                ps_sT = pps("ps_sT", [128, S], F32)
                ps_pv = pps("ps_pv", [HD + 1, S], F32)

                p.emit("gpsimd", lambda e: e.memset(KX[HD:HD + 1, :], 1.0), inc=sG)
                p.wait("gpsimd", tz_ready[0], 16)
                p.wait("vector", tz_ready[1], 32)

                for h in ([0] if DEBUG_L0 == 2 else range(H)):
                    tI, off64 = h // 2, (h % 2) * 64
                    p.wait("vector", sPE)
                    p.emit("vector", lambda e, tI=tI, off64=off64: e.tensor_copy(
                        out=KX[0:HD, :], in_=KT[tI][off64:off64 + HD, :]))
                    p.emit("vector", lambda e, tI=tI, off64=off64: e.tensor_copy(
                        out=QX[0:HD, :], in_=QT[tI][off64:off64 + HD, :]), inc=sV)
                    p.wait("scalar", sV)
                    p.emit("scalar", lambda e, tI=tI, off64=off64: e.activation(
                        out=nsq[:, :], in_=QT[tI][off64:off64 + HD, :], func=AF.Square), inc=sA)
                    p.wait("tensor", sA)
                    p.wait("tensor", sG)
                    for c4 in range(4):
                        p.emit("tensor", lambda e, c4=c4: e.matmul(
                            ps_pv[0:1, c4 * 512:c4 * 512 + 512], ones_col[0:HD, 0:1],
                            nsq[:, c4 * 512:(c4 + 1) * 512], start=True, stop=True),
                            inc=sPE if c4 == 3 else None)
                    p.wait("scalar", sPE)
                    p.emit("scalar", lambda e: e.activation(out=qn[0:1, :], in_=ps_pv[0:1, :],
                                                            func=AF.Sqrt), inc=sA)
                    p.emit("scalar", lambda e, tI=tI, off64=off64: e.activation(
                        out=nsq[:, :], in_=KT[tI][off64:off64 + HD, :], func=AF.Square), inc=sA)
                    p.wait("tensor", sA)
                    for c4 in range(4):
                        p.emit("tensor", lambda e, c4=c4: e.matmul(
                            ps_pv[0:1, c4 * 512:c4 * 512 + 512], ones_col[0:HD, 0:1],
                            nsq[:, c4 * 512:(c4 + 1) * 512], start=True, stop=True),
                            inc=sPE if c4 == 3 else None)
                    p.wait("vector", sPE)
                    if DEBUG_L0 and h == 0:
                        p.wait("sync", sPE)
                        p.wait("sync", sA)
                        p.dma("sync", dbg["d_nsq"], nsq[:, :], inc=sO)
                        p.emit("vector", lambda e: e.tensor_copy(out=rrec[0:1, :],
                                                                 in_=ps_pv[0:1, :]), inc=sV)
                        p.wait("sync", sV)
                        p.dma("sync", dbg["d_ksums"], rrec[0:1, :], inc=sO)
                        p.wait("vector", sO)
                    p.emit("vector", lambda e: e.reduce_max(out=hrow[0:1, 16:17],
                                                            in_=ps_pv[0:1, :], axis=AX.X), inc=sV)
                    p.wait("scalar", sV)
                    p.emit("scalar", lambda e: e.activation(out=hrow[0:1, 17:18],
                                                            in_=hrow[0:1, 16:17],
                                                            func=AF.Sqrt), inc=sA)
                    p.wait("vector", sA)
                    p.emit("vector", lambda e: e.tensor_scalar(out=hrow[0:1, 17:18],
                                                               in0=hrow[0:1, 17:18],
                                                               scalar1=-1.0, scalar2=None,
                                                               op0=ALU.mult), inc=sV)
                    p.emit("vector", lambda e, h=h: e.tensor_scalar(
                        out=hrow[0:1, 18:19], in0=hrow[0:1, h:h + 1],
                        scalar1=-8.0, scalar2=8.0 * SOFT_OFF, op0=ALU.mult, op1=ALU.add), inc=sV)
                    p.wait("vector", sV)
                    p.emit("vector", lambda e: e.scalar_tensor_tensor(
                        out=QX[HD:HD + 1, :], in0=qn[0:1, :], scalar=hrow[0:1, 17:18],
                        in1=hrow[0:1, 18:19].broadcast_to([1, S]),
                        op0=ALU.mult, op1=ALU.add), inc=sV)
                    p.wait("tensor", sV)
                    if DEBUG_L0:
                        p.wait("sync", sV)
                        p.dma("sync", dbg["d_qxrows"][h:h + 1, :], QX[:, :][0:HD + 1, :][HD:HD + 1, :], inc=sO)
                        p.dma("sync", dbg["d_hrows"][h:h + 1, :], hrow[0:1, 0:64], inc=sO)
                        p.wait("vector", sO)

                    for kt in range(NT):
                        p.wait("vector", sB)
                        c0 = h * (2 * S - 1) + (S - 1) - 128 * kt
                        p.emit("vector", lambda e, c0=c0: e.tensor_scalar(
                            out=idxt[:], in0=idxbase[:], scalar1=c0, scalar2=None,
                            op0=ALU.add), inc=sV)
                        p.wait("gpsimd", sV)
                        p.emit("gpsimd", lambda e: e.indirect_dma_start(
                            out=bias0[:, :], out_offset=None,
                            in_=bass.AP(tzrev, 0, [[1, H * (2 * S - 1)], [1, 1]]),
                            in_offset=IndirectOffsetOnAxis(ap=idxt[:, 0:1], axis=0)),
                            inc=sB, inc_n=16)
                        if kt >= 1:
                            p.wait("tensor", sV)
                        for c4 in range(4):
                            p.emit("tensor", lambda e, kt=kt, c4=c4: e.matmul(
                                ps_sT[:, c4 * 512:(c4 + 1) * 512],
                                KX[:, kt * 128:(kt + 1) * 128],
                                QX[:, c4 * 512:(c4 + 1) * 512],
                                start=True, stop=True),
                                inc=sPE if c4 == 3 else None)
                        p.wait("vector", sPE)
                        p.wait("vector", sB)
                        if kt >= 1:
                            p.wait("vector", sA)
                        p.emit("vector", lambda e, kt=kt: e.scalar_tensor_tensor(
                            out=scArg[:, :], in0=ps_sT[:, :], scalar=0.125,
                            in1=bias0[:, :], op0=ALU.mult, op1=ALU.add), inc=sV)
                        p.wait("scalar", sV)
                        if kt >= 2:
                            p.wait("scalar", sPE)
                        p.emit("scalar", lambda e, kt=kt: e.activation(
                            out=WT[kt % 2][:, :], in_=scArg[:, :], func=AF.Exp), inc=sA)
                        p.wait("tensor", sA)
                        for c4 in range(4):
                            p.emit("tensor", lambda e, kt=kt, c4=c4, h=h: e.matmul(
                                ps_pv[:, c4 * 512:(c4 + 1) * 512],
                                Vst[kt][:, h * (HD + 1):(h + 1) * (HD + 1)],
                                WT[kt % 2][:, c4 * 512:(c4 + 1) * 512],
                                start=(kt == 0), stop=(kt == NT - 1)),
                                inc=sPE if c4 == 3 else None)

                    p.wait("vector", sPE)
                    if DEBUG_L0:
                        p.wait("sync", sPE)
                        p.dma("sync", dbg["d_rsum"][h:h + 1, :], ps_pv[HD:HD + 1, :], inc=sO) \
                            if False else None
                        p.emit("vector", lambda e: e.tensor_copy(out=rrec[0:1, :],
                                                                 in_=ps_pv[HD:HD + 1, :]), inc=sV)
                        p.wait("sync", sV)
                        p.dma("sync", dbg["d_rsum"][h:h + 1, :], rrec[0:1, :], inc=sO)
                        p.wait("vector", sO)
                    p.emit("vector", lambda e: e.reciprocal(out=rrec[0:1, :],
                                                            in_=ps_pv[HD:HD + 1, :]), inc=sV)
                    p.wait("tensor", sV)
                    for c4 in range(4):
                        p.emit("tensor", lambda e, c4=c4: e.matmul(
                            ps_sT[0:64, c4 * 512:(c4 + 1) * 512], ones_row_f[0:1, 0:64],
                            rrec[0:1, c4 * 512:(c4 + 1) * 512], start=True, stop=True),
                            inc=sPE if c4 == 3 else None)
                    p.wait("vector", sPE)
                    p.emit("vector", lambda e: e.tensor_copy(out=rb[:, :], in_=ps_sT[0:64, :]),
                           inc=sV)
                    p.emit("vector", lambda e, tI=tI, off64=off64: e.tensor_tensor(
                        out=attnT[tI][off64:off64 + HD, :], in0=ps_pv[0:HD, :], in1=rb[:, :],
                        op=ALU.mult), inc=sV)
                    p.wait("tensor", sV)
                    if DEBUG_L0 and h == (0 if DEBUG_L0 == 2 else H - 1):
                        p.wait("sync", sV)
                        p.wait("sync", sA)
                        p.wait("sync", sB)
                        p.dma("sync", dbg["d_scarg"], scArg[:, :], inc=sO)
                        p.wait("vector", sO)
                        p.emit("vector", lambda e: e.tensor_copy(out=scArg[0:HD + 1, :],
                                                                 in_=ps_pv[:, :]), inc=sV)
                        p.wait("sync", sV)
                        p.dma("sync", dbg["d_pv"], scArg[0:HD + 1, :], inc=sO)
                        p.dma("sync", dbg["d_KT0"], KT[0][:, :], inc=sO)
                        p.dma("sync", dbg["d_QT0"], QT[0][:, :], inc=sO)
                        p.dma("sync", dbg["d_V0"], Vst[0][:, :], inc=sO)
                        p.dma("sync", dbg["d_QX"], QX[:, :], inc=sO)
                        p.dma("sync", dbg["d_qn"], qn[:, :], inc=sO)
                        p.dma("sync", dbg["d_bias"], bias0[:, :], inc=sO)
                        p.dma("sync", dbg["d_wt"], WT[(NT - 1) % 2][:, :], inc=sO)
                        p.dma("sync", dbg["d_attnT0"], attnT[0][:, :], inc=sO)
                        p.dma("sync", dbg["d_tzrev"], tzrev.ap(), inc=sO)
                        p.dma("sync", dbg["d_hrow"], hrow[0:1, 0:64], inc=sO)
                p.wait("sync", sO)
                p.wait("sync", sB)
                p.barrier("c%d" % b)

    def _phase_d(b, ph, attnT):
        psb = lambda name, shape, dt: ph.enter_context(nc.sbuf_tensor(_uname(name), shape, dt))
        pps = lambda name, shape, dt: ph.enter_context(nc.psum_tensor(_uname(name), shape, dt))
        if True:
            if True:
                xt = psb("xt", [128, D], F32)
                sqs = psb("sqs", [128, D], F32)
                st = psb("st", [128, 8], F32)
                wg = [psb(f"wg{i}", [128, 2 * D], BF16) for i in range(DT)]
                wgf = psb("wgf", [128, 512], F32)
                gs = psb("gs", [128, 2 * D], F32)
                ao = psb("ao", [128, D], F32)
                lno = psb("lno", [128, D], F32)
                bgb = psb("bgb", [128, 2 * D], F32)
                ps_g = pps("ps_g", [128, 2 * D], F32)

                p.emit("vector", lambda e: e.memset(st[:, 7:8], EPS), inc=sV)
                sBg = p.sem("sBg")
                with ExitStack() as bctx:
                    bgrow = bctx.enter_context(nc.sbuf_tensor(_uname("bgrow"), [1, 512], F32))
                    p.wait("tensor", sG)
                    for c4 in range(4):
                        p.wait("sync", sPE)
                        p.dma("sync", bgrow[0:1, :], b_gate[0:1, c4 * 512:(c4 + 1) * 512],
                              inc=sBg)
                        p.wait("tensor", sBg)
                        p.wait("tensor", sV)
                        p.emit("tensor", lambda e, c4=c4: e.matmul(
                            ps_g[:, (c4 % 2) * 512:(c4 % 2) * 512 + 512],
                            ones_row_f[0:1, 0:128],
                            bgrow[0:1, :], start=True, stop=True), inc=sPE)
                        p.wait("vector", sPE)
                        p.emit("vector", lambda e, c4=c4: e.tensor_copy(
                            out=bgb[:, c4 * 512:(c4 + 1) * 512],
                            in_=ps_g[:, (c4 % 2) * 512:(c4 % 2) * 512 + 512]), inc=sV)
                    p.wait("tensor", sV)

                for kt in range(DT):
                    for cc in range(4):
                        p.wait("sync", sA)
                        p.dma("sync", wgf[:, :],
                              w_gate[kt * 128:(kt + 1) * 128, cc * 512:(cc + 1) * 512], inc=sD)
                        p.wait("scalar", sD)
                        p.emit("scalar", lambda e, kt=kt, cc=cc: e.activation(
                            out=wg[kt][:, cc * 512:(cc + 1) * 512], in_=wgf[:, :],
                            func=AF.Copy), inc=sA)
                p.wait("tensor", sA)

                for qt in range(NT):
                    for c4 in range(4):
                        for kt in range(DT):
                            p.emit("tensor", lambda e, qt=qt, c4=c4, kt=kt: e.matmul(
                                ps_g[:, c4 * 512:(c4 + 1) * 512],
                                attnT[kt][:, qt * 128:(qt + 1) * 128],
                                wg[kt][:, c4 * 512:(c4 + 1) * 512],
                                start=(kt == 0), stop=(kt == DT - 1)),
                                inc=sPE if (kt == DT - 1 and c4 == 3) else None)
                    p.wait("vector", sPE)
                    p.emit("vector", lambda e: e.tensor_tensor(out=gs[:, :], in0=ps_g[:, :],
                                                               in1=bgb[:, :], op=ALU.add),
                           inc=sV)
                    p.wait("tensor", sV)
                    p.wait("scalar", sV)
                    p.emit("scalar", lambda e: e.activation(out=sqs[:, :], in_=gs[:, D:2 * D],
                                                            func=AF.Sigmoid), inc=sA)
                    p.wait("vector", sA)
                    p.emit("vector", lambda e: e.tensor_tensor(out=ao[:, :], in0=gs[:, 0:D],
                                                               in1=sqs[:, :], op=ALU.mult),
                           inc=sV)
                    p.wait("scalar", sV)
                    ln_emit(p, ao[:, :], lno[:, :], sqs, st, sA, sV)
                    p.wait("sync", sV)
                    p.dma("sync", xt[:, :], x_in[b, qt * 128:(qt + 1) * 128, :], inc=sD)
                    p.wait("vector", sD)
                    p.emit("vector", lambda e: e.tensor_tensor(out=lno[:, :], in0=lno[:, :],
                                                               in1=xt[:, :], op=ALU.add),
                           inc=sV)
                    p.wait("sync", sV)
                    p.dma("sync", x_out[b, qt * 128:(qt + 1) * 128, :], lno[:, :], inc=sO)
                    p.wait("vector", sO)
                p.wait("sync", sO)
                p.barrier("d%d" % b)

    for b in ([0] if DEBUG_L0 else range(BPC)):
        with ExitStack() as ph:
            _phase_ab(b, ph)
        with ExitStack() as cd:
            attnT = [cd.enter_context(nc.sbuf_tensor(_uname("attnT"), [128, S], BF16))
                     for i in range(DT)]
            with ExitStack() as ph:
                _phase_c(b, ph, attnT)
            with ExitStack() as ph:
                _phase_d(b, ph, attnT)


    p.wait("sync", sO)
    if own:
        p.realize()
    top.close()
    return p



DH = 4096


def build_l0ffn(nc, p=None, xin_ap=None, xout_ap=None, shared=None):
    own = p is None
    ctx = nc.ctx
    x_in = xin_ap if xin_ap is not None else \
        nc.dram_tensor("x1", [BPC, S, D], F32, kind="ExternalInput").ap()
    x_out = xout_ap if xout_ap is not None else \
        nc.dram_tensor("x2", [BPC, S, D], F32, kind="ExternalOutput").ap()
    wdr = {}
    for nm, (o_, i_) in (("w1", (DH, D)), ("w2", (DH, D)), ("h1", (DH, DH)),
                         ("h2", (DH, DH)), ("w3", (D, DH))):
        wdr[nm] = nc.dram_tensor(nm + "_w", [o_, i_], F32, kind="ExternalInput").ap()
        wdr[nm + "_s"] = nc.dram_tensor(nm + "_s", [1, o_], F32, kind="ExternalInput").ap()
        wdr[nm + "T"] = nc.dram_tensor(nm + "T_scr", [i_, o_], BF16, kind="Internal")

    top = ExitStack()
    sb = lambda name, shape, dt: top.enter_context(nc.sbuf_tensor(_uname(name), shape, dt))
    if shared is None:
        ident = sb("ident", [128, 128], BF16)
        identf = sb("identf", [128, 128], F32)
    else:
        ident, identf = shared["ident"], shared["identf"]

    if own:
        p = Prog(nc, ctx)
    sD = p.sem("sD")
    sPE = p.sem("sPE")
    sV = p.chain["vector"]
    sA = p.chain["scalar"]
    sG = p.chain["gpsimd"]
    sO = p.sem("sO")

    if shared is None:
        p.emit("gpsimd", lambda e: e.memset(ident[:], 1.0))
        p.emit("gpsimd", lambda e: e.affine_select(out=ident[:], in_=ident[:],
                                                   pattern=[[1, 128]],
                                                   compare_op=ALU.is_equal, fill=0.0, base=0,
                                                   channel_multiplier=-1))
        p.emit("gpsimd", lambda e: e.memset(identf[:], 1.0))
        p.emit("gpsimd", lambda e: e.affine_select(out=identf[:], in_=identf[:],
                                                   pattern=[[1, 128]],
                                                   compare_op=ALU.is_equal, fill=0.0, base=0,
                                                   channel_multiplier=-1), inc=sG)
    consts_g = sG.n

    def _prep(ph):
        psb = lambda name, shape, dt: ph.enter_context(nc.sbuf_tensor(_uname(name), shape, dt))
        pps = lambda name, shape, dt: ph.enter_context(nc.psum_tensor(_uname(name), shape, dt))
        wf = psb("wf", [128, DH], F32)
        wb = psb("wb", [128, DH], BF16)
        tb = psb("tb", [128, 1024], BF16)
        sv = psb("sv", [128, 1], F32)
        ps_t = pps("ps_t", [128, 1024], BF16)
        sWl = p.sem("sWl")
        sSl = p.sem("sSl")
        sTo = p.sem("sTo")
        p.wait("tensor", sG, consts_g)
        for nm, (o_, i_) in (("w1", (DH, D)), ("w2", (DH, D)), ("h1", (DH, DH)),
                             ("h2", (DH, DH)), ("w3", (D, DH))):
            w_ap = wdr[nm]
            s_ap = wdr[nm + "_s"]
            scr = wdr[nm + "T"]
            for ot in range(o_ // 128):
                p.wait("sync", sA)
                p.dma("sync", wf[:, 0:i_], w_ap[ot * 128:(ot + 1) * 128, :], inc=sWl)
                p.dma("sync", sv[:, 0:1],
                      bass.AP(s_ap.tensor, ot * 128, [[1, 128], [1, 1]]), inc=sSl)
                p.wait("scalar", sWl)
                p.wait("scalar", sSl)
                p.emit("scalar", lambda e, i_=i_: e.activation(
                    out=wb[:, 0:i_], in_=wf[:, 0:i_], func=AF.Copy, scale=sv[:, 0:1]),
                    inc=sA)
                p.wait("tensor", sA)
                for jb in range(i_ // 1024):
                    p.wait("tensor", sA)
                    for j8 in range(8):
                        p.emit("tensor", lambda e, jb=jb, j8=j8: e.transpose(
                            ps_t[:, j8 * 128:(j8 + 1) * 128],
                            wb[:, (jb * 8 + j8) * 128:(jb * 8 + j8 + 1) * 128],
                            ident[:]), inc=sPE if j8 == 7 else None)
                    p.wait("scalar", sPE)
                    p.wait("scalar", sTo)
                    p.emit("scalar", lambda e: e.activation(out=tb[:, :], in_=ps_t[:, :],
                                                            func=AF.Copy), inc=sA)
                    p.wait("sync", sA)
                    p.dma("sync",
                          bass.AP(scr, jb * 1024 * o_ + ot * 128,
                                  [[o_, 128], [128 * o_, 8], [1, 128]]),
                          tb[:, :].rearrange("p (a b) -> p a b", b=128), inc=sTo)
        p.wait("sync", sTo)
        p.wait("sync", sWl)
        p.wait("sync", sSl)
        p.barrier("prep")

    NC_ = 512
    NCH = BPC * S // NC_

    def _chunk(c, ph):
        psb = lambda name, shape, dt: ph.enter_context(nc.sbuf_tensor(_uname(name), shape, dt))
        pps = lambda name, shape, dt: ph.enter_context(nc.psum_tensor(_uname(name), shape, dt))
        xt = psb("xt", [128, D], F32)
        sqs = psb("sqs", [128, D], F32)
        st = psb("st", [128, 8], F32)
        ln2b = psb("ln2b", [128, D], BF16)
        ln2T = [psb(f"ln2T{i}", [128, NC_], BF16) for i in range(DT)]
        s1 = psb("s1", [128, 32 * NC_], BF16)
        hh = psb("hh", [128, 32 * NC_], BF16)
        g1 = psb("g1", [128, 32 * NC_], BF16)
        wbuf = [psb(f"wbuf{i}", [128, DH], BF16) for i in range(2)]
        foT = psb("foT", [128, D], F32)
        lno = psb("lno", [128, D], F32)
        ps_m = pps("ps_m", [128, NC_], F32)
        ps_m2 = pps("ps_m2", [128, NC_], F32)
        ps_t = pps("ps_t", [128, 1024], F32)
        ps_tb = pps("ps_tb", [128, 1024], BF16)
        sW = [p.sem("sW0"), p.sem("sW1")]
        sX = p.sem("sX")

        def tile_sl(buf, t):
            return buf[:, :].rearrange("p (a b) -> p a b", b=NC_)[:, t, :]

        p.emit("vector", lambda e: e.memset(st[:, 7:8], EPS), inc=sV)
        for it in range(NC_ // 128):
            tok = c * NC_ + it * 128
            b, s0 = tok // S, tok % S
            p.wait("sync", sV)
            p.wait("sync", sA)
            p.dma("sync", xt[:, :], x_in[b, s0:s0 + 128, :], inc=sX)
            p.wait("vector", sX)
            p.wait("scalar", sX)
            p.wait("vector", sPE)
            ln_emit(p, xt[:, :], ln2b[:, :], sqs, st, sA, sV)
            p.wait("tensor", sV)
            p.wait("tensor", sA)
            for j in range(DT):
                p.emit("tensor", lambda e, j=j: e.transpose(
                    ps_tb[:, j * 128:(j + 1) * 128], ln2b[:, j * 128:(j + 1) * 128],
                    ident[:]), inc=sPE if j == DT - 1 else None)
            p.wait("scalar", sPE)
            for j in range(DT):
                p.emit("scalar", lambda e, j=j, it=it: e.activation(
                    out=ln2T[j][:, it * 128:(it + 1) * 128],
                    in_=ps_tb[:, j * 128:(j + 1) * 128], func=AF.Copy),
                    inc=sA if j == DT - 1 else None)
            p.wait("tensor", sA)
        lnT_a = sA.n


        def evac_b(ot, ps_u):
            p.wait("scalar", sPE)
            p.emit("scalar", lambda e, ot=ot, ps_u=ps_u: e.activation(
                out=tile_sl(s1, ot), in_=ps_u[:, :], func=AF.Silu), inc=sA)
        def run_stage(src_tiles, wscr, KI, o_dim, evac):
            entry_pe = sPE.n
            mm_counts = {0: entry_pe, 1: entry_pe}
            evac_counts = {}
            entry_a, entry_v = sA.n, sV.n
            for ot in range(o_dim // 128):
                bi = ot % 2
                if True:
                    p.wait("sync", sPE, mm_counts[ot % 2])
                p.dma("sync", wbuf[bi][:, 0:KI * 128].rearrange("p (a b) -> p a b", b=128),
                      bass.AP(wscr, ot * 128, [[o_dim, 128], [128 * o_dim, KI], [1, 128]]),
                      inc=sW[bi])
                p.wait("tensor", sW[bi])
                ps_u = ps_m if (ot % 2 == 0) else ps_m2
                if ot >= 2:
                    p.wait("tensor", sA, evac_counts[ot % 2])
                else:
                    p.wait("tensor", sA, entry_a)
                    p.wait("tensor", sV, entry_v)
                for nh in range(NC_ // 512):
                    for it in range(KI):
                        p.emit("tensor", lambda e, bi=bi, nh=nh, it=it, ps_u=ps_u, src_tiles=src_tiles: e.matmul(
                            ps_u[:, nh * 512:(nh + 1) * 512],
                            wbuf[bi][:, it * 128:(it + 1) * 128],
                            src_tiles(it)[:, nh * 512:(nh + 1) * 512],
                            start=(it == 0), stop=(it == KI - 1)),
                            inc=sPE if (it == KI - 1 and nh == NC_ // 512 - 1) else None)
                mm_counts[ot % 2] = sPE.n
                evac(ot, ps_u)
                evac_counts[ot % 2] = sA.n

        p.wait("tensor", sA, lnT_a)
        run_stage(lambda it: ln2T[it], wdr["w1T"], DT, DH, evac_b)

        def evac_c(ot, ps_u):
            p.wait("vector", sPE)
            p.wait("vector", sA)
            p.emit("vector", lambda e, ot=ot, ps_u=ps_u: e.tensor_tensor(
                out=tile_sl(hh, ot), in0=ps_u[:, :], in1=tile_sl(s1, ot), op=ALU.mult),
                inc=sV)
        def run_stage_v(src_tiles, wscr, KI, o_dim, evac):
            entry_pe = sPE.n
            mm_counts = {0: entry_pe, 1: entry_pe}
            evac_counts = {}
            entry_a, entry_v = sA.n, sV.n
            for ot in range(o_dim // 128):
                bi = ot % 2
                if True:
                    p.wait("sync", sPE, mm_counts[ot % 2])
                p.dma("sync", wbuf[bi][:, 0:KI * 128].rearrange("p (a b) -> p a b", b=128),
                      bass.AP(wscr, ot * 128, [[o_dim, 128], [128 * o_dim, KI], [1, 128]]),
                      inc=sW[bi])
                p.wait("tensor", sW[bi])
                ps_u = ps_m if (ot % 2 == 0) else ps_m2
                if ot >= 2:
                    p.wait("tensor", sV, evac_counts[ot % 2])
                else:
                    p.wait("tensor", sA, entry_a)
                    p.wait("tensor", sV, entry_v)
                for nh in range(NC_ // 512):
                    for it in range(KI):
                        p.emit("tensor", lambda e, bi=bi, nh=nh, it=it, ps_u=ps_u, src_tiles=src_tiles: e.matmul(
                            ps_u[:, nh * 512:(nh + 1) * 512],
                            wbuf[bi][:, it * 128:(it + 1) * 128],
                            src_tiles(it)[:, nh * 512:(nh + 1) * 512],
                            start=(it == 0), stop=(it == KI - 1)),
                            inc=sPE if (it == KI - 1 and nh == NC_ // 512 - 1) else None)
                mm_counts[ot % 2] = sPE.n
                evac(ot, ps_u)
                evac_counts[ot % 2] = sV.n

        run_stage_v(lambda it: ln2T[it], wdr["w2T"], DT, DH, evac_c)

        def evac_d(ot, ps_u):
            p.wait("scalar", sPE)
            p.wait("scalar", sV)
            p.emit("scalar", lambda e, ot=ot, ps_u=ps_u: e.activation(
                out=tile_sl(g1, ot), in_=ps_u[:, :], func=AF.Gelu), inc=sA)
        run_stage(lambda it: tile_sl(hh, it), wdr["h1T"], 32, DH, evac_d)

        def evac_e(ot, ps_u):
            p.wait("scalar", sPE)
            p.emit("scalar", lambda e, ot=ot, ps_u=ps_u: e.activation(
                out=tile_sl(s1, ot), in_=ps_u[:, :], func=AF.Gelu), inc=sA)
        run_stage(lambda it: tile_sl(g1, it), wdr["h2T"], 32, DH, evac_e)

        fo2 = [psb(f"fo2_{i}", [128, NC_], F32) for i in range(DT)]
        def evac_f2(ot, ps_u):
            p.wait("vector", sPE)
            p.emit("vector", lambda e, ot=ot, ps_u=ps_u: e.tensor_copy(
                out=fo2[ot][:, :], in_=ps_u[:, :]), inc=sV)
        run_stage_v(lambda it: tile_sl(s1, it), wdr["w3T"], 32, D, evac_f2)

        for it in range(NC_ // 128):
            tok = c * NC_ + it * 128
            b, s0 = tok // S, tok % S
            p.wait("tensor", sV)
            p.wait("tensor", sA)
            for j in range(DT):
                p.emit("tensor", lambda e, j=j, it=it: e.transpose(
                    ps_t[:, j * 128:(j + 1) * 128],
                    fo2[j][:, it * 128:(it + 1) * 128], identf[:]),
                    inc=sPE if j == DT - 1 else None)
            p.wait("vector", sPE)
            p.emit("vector", lambda e: e.tensor_copy(out=foT[:, :], in_=ps_t[:, 0:D]), inc=sV)
            p.wait("vector", sV)
            p.wait("scalar", sV)
            ln_emit(p, foT[:, :], lno[:, :], sqs, st, sA, sV)
            p.wait("sync", sV)
            p.wait("sync", sA)
            p.dma("sync", xt[:, :], x_in[b, s0:s0 + 128, :], inc=sX)
            p.wait("vector", sX)
            p.emit("vector", lambda e: e.tensor_tensor(out=lno[:, :], in0=lno[:, :],
                                                       in1=xt[:, :], op=ALU.add), inc=sV)
            p.wait("sync", sV)
            p.dma("sync", x_out[b, s0:s0 + 128, :], lno[:, :], inc=sO)
            p.wait("vector", sO)
        p.wait("sync", sO)
        p.barrier(f"ch{c}")

    with ExitStack() as ph:
        _prep(ph)
    for c in range(NCH):
        with ExitStack() as ph:
            _chunk(c, ph)

    p.wait("sync", sO)
    if own:
        p.realize()
    top.close()
    return p


def _l0ffn_launch(x1, w1_w, w1_s, w2_w, w2_s, h1_w, h1_s, h2_w, h2_s, w3_w, w3_s):
    nc = bass.Bass("TRN2", target_bir_lowering=False)
    build_l0ffn(nc)
    in_maps = []
    for c in range(NCORES):
        in_maps.append({
            "x1": np.ascontiguousarray(x1[c * BPC:(c + 1) * BPC]),
            "w1_w": w1_w, "w1_s": np.ascontiguousarray(w1_s.reshape(1, -1)),
            "w2_w": w2_w, "w2_s": np.ascontiguousarray(w2_s.reshape(1, -1)),
            "h1_w": h1_w, "h1_s": np.ascontiguousarray(h1_s.reshape(1, -1)),
            "h2_w": h2_w, "h2_s": np.ascontiguousarray(h2_s.reshape(1, -1)),
            "w3_w": w3_w, "w3_s": np.ascontiguousarray(w3_s.reshape(1, -1)),
        })
    res = run_bass_kernel_spmd(nc, in_maps, core_ids=list(range(NCORES)))
    return np.concatenate([res.results[c]["x2"] for c in range(NCORES)], axis=0)




def build_l1attn(nc, p=None, xin_ap=None, shared=None):
    own = p is None
    ctx = nc.ctx
    x_in = xin_ap if xin_ap is not None else \
        nc.dram_tensor("x2", [BPC, S, D], F32, kind="ExternalInput").ap()
    sig = nc.dram_tensor("r_sigma", [1, 1], F32, kind="ExternalInput").ap()
    gw = nc.dram_tensor("gate_weight", [8, D], F32, kind="ExternalInput").ap()
    x_out = nc.dram_tensor("x3", [BPC, S, D], F32, kind="ExternalOutput").ap()
    lg_out = nc.dram_tensor("logitsT", [8, BPC * S], F32, kind="ExternalOutput").ap()

    top = ExitStack()
    sb = lambda name, shape, dt: top.enter_context(nc.sbuf_tensor(_uname(name), shape, dt))
    if shared is None:
        ident = sb("ident", [128, 128], BF16)
        ones_row_f = sb("ones_row_f", [1, 128], F32)
    else:
        ident, ones_row_f = shared["ident"], shared["ones_row_f"]
    xt = sb("xt", [128, D], F32)
    sqs = sb("sqs", [128, D], F32)
    st = sb("st", [128, 8], F32)
    ln3f = sb("ln3f", [128, D], F32)
    ipm = sb("ipm", [128, H * H], F32)
    dg = sb("dg", [128, H], F32)
    sumdg = sb("sumdg", [128, H * H], F32)
    kmat = sb("kmat", [128, H * H], F32)
    mixb = sb("mixb", [128, D], F32)
    acc = sb("acc", [128, D], F32)
    lno = sb("lno", [128, D], F32)
    ln4b = sb("ln4b", [128, D], BF16)
    ln4T = sb("ln4T", [128, D], BF16)
    gwb = sb("gwb", [8, D], BF16)
    gwf = sb("gwf", [8, D], F32)
    gwT = sb("gwT", [128, 8 * DT], BF16)
    sgm = sb("sgm", [128, 4], F32)
    sgr = sb("sgr", [1, 4], F32)
    lrow = sb("lrow", [8, 128], F32)
    ps_t = top.enter_context(nc.psum_tensor(_uname("ps_tb"), [128, D], BF16))
    ps_s = top.enter_context(nc.psum_tensor(_uname("ps_s"), [128, 4], F32))
    ps_l = top.enter_context(nc.psum_tensor(_uname("ps_l"), [8, 128], F32))

    if own:
        p = Prog(nc, ctx)
    sD = p.sem("sD")
    sPE = p.sem("sPE")
    sV = p.chain["vector"]
    sA = p.chain["scalar"]
    sG = p.chain["gpsimd"]
    sO = p.sem("sO")
    sSg = p.sem("sSg")

    if shared is None:
        p.emit("gpsimd", lambda e: e.memset(ident[:], 1.0))
        p.emit("gpsimd", lambda e: e.affine_select(out=ident[:], in_=ident[:],
                                                   pattern=[[1, 128]],
                                                   compare_op=ALU.is_equal, fill=0.0, base=0,
                                                   channel_multiplier=-1))
        p.emit("gpsimd", lambda e: e.memset(ones_row_f[:], 1.0), inc=sG)
    p.emit("vector", lambda e: e.memset(st[:, 7:8], EPS), inc=sV)
    p.dma("sync", sgr[0:1, 0:1], sig[0:1, 0:1], inc=sSg)
    p.dma("sync", gwf[0:8, :], gw[:, :], inc=sSg)
    p.wait("tensor", sSg)
    p.wait("tensor", sG)
    p.emit("tensor", lambda e: e.matmul(ps_s[:, 0:1], ones_row_f[0:1, 0:128],
                                        sgr[0:1, 0:1], start=True, stop=True), inc=sPE)
    p.wait("vector", sPE)
    p.emit("vector", lambda e: e.tensor_scalar(out=sgm[:, 0:1], in0=ps_s[:, 0:1],
                                               scalar1=2.0, scalar2=None, op0=ALU.mult), inc=sV)
    p.emit("vector", lambda e: e.tensor_scalar(out=sgm[:, 1:2], in0=ps_s[:, 0:1],
                                               scalar1=-1.0, scalar2=None, op0=ALU.mult), inc=sV)
    p.wait("scalar", sSg)
    p.emit("scalar", lambda e: e.activation(out=gwb[0:8, :], in_=gwf[0:8, :], func=AF.Copy),
           inc=sA)
    p.wait("tensor", sA)
    for j in range(DT):
        p.emit("tensor", lambda e, j=j: e.transpose(
            ps_t[:, j * 128:j * 128 + 8], gwb[0:8, j * 128:(j + 1) * 128],
            ident[0:8, 0:8]), inc=sPE if j == DT - 1 else None)
    p.wait("scalar", sPE)
    for j in range(DT):
        p.emit("scalar", lambda e, j=j: e.activation(
            out=gwT[:, j * 8:(j + 1) * 8], in_=ps_t[:, j * 128:j * 128 + 8], func=AF.Copy),
            inc=sA if j == DT - 1 else None)
    gw_ready = sA.n

    for tile in range(BPC * S // 128):
        b, s0 = (tile * 128) // S, (tile * 128) % S
        p.wait("sync", sV)
        p.wait("sync", sA)
        p.dma("sync", xt[:, :], x_in[b, s0:s0 + 128, :], inc=sD)
        p.wait("vector", sD)
        p.wait("scalar", sD)
        ln_emit(p, xt[:, :], ln3f[:, :], sqs, st, sA, sV)
        for h in range(H):
            g0 = H - h
            xh = ln3f[:, h * HD:(h + 1) * HD].rearrange("p (a d) -> p a d", d=HD)
            xh_b = xh.broadcast_to([128, g0, HD])
            rest = ln3f[:, h * HD:D].rearrange("p (g d) -> p g d", d=HD)
            p.emit("vector", lambda e, xh_b=xh_b, rest=rest, g0=g0: e.tensor_tensor(
                out=mixb[:, 0:g0 * HD].rearrange("p (g d) -> p g d", d=HD),
                in0=xh_b, in1=rest, op=ALU.mult), inc=sV)
            p.emit("vector", lambda e, h=h, g0=g0: e.tensor_reduce(
                out=ipm[:, h * H + h:h * H + H],
                in_=mixb[:, 0:g0 * HD].rearrange("p (g d) -> p g d", d=HD),
                axis=AX.X, op=ALU.add), inc=sV)
        for h in range(H - 1):
            cnt = H - 1 - h
            p.emit("vector", lambda e, h=h, cnt=cnt: e.tensor_copy(
                out=bass.AP(ipm.tensor if hasattr(ipm, "tensor") else ipm, 0, [[0, 0]])
                if False else
                bass.AP(ipm, (h + 1) * H + h, [[H * H, 128], [H, cnt]]),
                in_=ipm[:, h * H + h + 1:h * H + H]), inc=sV)
        p.emit("vector", lambda e: e.tensor_copy(
            out=dg[:, 0:H], in_=bass.AP(ipm, 0, [[H * H, 128], [H + 1, H]])), inc=sV)
        p.emit("vector", lambda e: e.tensor_tensor(
            out=sumdg[:, :].rearrange("p (a b) -> p a b", b=H),
            in0=dg[:, :].rearrange("p (a o) -> p a o", o=1).broadcast_to([128, H, H]),
            in1=dg[:, :].rearrange("p (o b) -> p o b", o=1).broadcast_to([128, H, H]),
            op=ALU.add), inc=sV)
        p.emit("vector", lambda e: e.tensor_scalar(out=kmat[:, :], in0=ipm[:, :],
                                                   scalar1=sgm[:, 0:1], scalar2=None,
                                                   op0=ALU.mult), inc=sV)
        p.emit("vector", lambda e: e.scalar_tensor_tensor(
            out=kmat[:, :], in0=sumdg[:, :], scalar=sgm[:, 1:2], in1=kmat[:, :],
            op0=ALU.mult, op1=ALU.add), inc=sV)
        p.wait("scalar", sV)
        p.emit("scalar", lambda e: e.activation(out=kmat[:, :], in_=kmat[:, :],
                                                func=AF.Exp), inc=sA)
        p.wait("vector", sA)
        p.emit("vector", lambda e: e.tensor_copy(out=acc[:, :], in_=ln3f[:, :]), inc=sV)
        for g in range(H):
            xg = ln3f[:, g * HD:(g + 1) * HD].rearrange("p (o d) -> p o d", o=1)
            xg_b = xg.broadcast_to([128, H, HD])
            kv = bass.AP(kmat, g, [[H * H, 128], [H, H], [0, HD]])
            p.emit("vector", lambda e, xg_b=xg_b, kv=kv: e.tensor_tensor(
                out=mixb[:, :].rearrange("p (a d) -> p a d", d=HD),
                in0=xg_b, in1=kv, op=ALU.mult), inc=sV)
            p.emit("vector", lambda e: e.tensor_tensor(out=acc[:, :], in0=acc[:, :],
                                                       in1=mixb[:, :], op=ALU.add), inc=sV)
        p.wait("scalar", sV)
        ln_emit(p, acc[:, :], lno[:, :], sqs, st, sA, sV)
        p.wait("vector", sV)
        p.emit("vector", lambda e: e.tensor_tensor(out=lno[:, :], in0=lno[:, :],
                                                   in1=xt[:, :], op=ALU.add), inc=sV)
        p.wait("sync", sV)
        p.dma("sync", x_out[b, s0:s0 + 128, :], lno[:, :], inc=sO)
        p.wait("scalar", sV)
        ln_emit(p, lno[:, :], ln4b[:, :], sqs, st, sA, sV)
        p.wait("tensor", sV)
        p.wait("tensor", sA, gw_ready)
        p.wait("tensor", sA)
        for j in range(DT):
            p.emit("tensor", lambda e, j=j: e.transpose(
                ps_t[:, j * 128:(j + 1) * 128], ln4b[:, j * 128:(j + 1) * 128],
                ident[:]), inc=sPE if j == DT - 1 else None)
        p.wait("scalar", sPE)
        for j in range(DT):
            p.emit("scalar", lambda e, j=j: e.activation(
                out=ln4T[:, j * 128:(j + 1) * 128], in_=ps_t[:, j * 128:(j + 1) * 128],
                func=AF.Copy), inc=sA if j == DT - 1 else None)
        p.wait("tensor", sA)
        p.wait("tensor", sV)
        for j in range(DT):
            p.emit("tensor", lambda e, j=j: e.matmul(
                ps_l[:, :], gwT[:, j * 8:(j + 1) * 8], ln4T[:, j * 128:(j + 1) * 128],
                start=(j == 0), stop=(j == DT - 1)), inc=sPE if j == DT - 1 else None)
        p.wait("vector", sPE)
        p.emit("vector", lambda e: e.tensor_copy(out=lrow[0:8, :], in_=ps_l[:, :]), inc=sV)
        p.wait("sync", sV)
        p.dma("sync", lg_out[:, tile * 128:(tile + 1) * 128], lrow[0:8, :], inc=sO)
        p.wait("vector", sO)

    p.wait("sync", sO)
    if own:
        p.realize()
    top.close()
    return p


def _l1attn_launch(x2, r_sigma, gate_weight):
    nc = bass.Bass("TRN2", target_bir_lowering=False)
    build_l1attn(nc)
    in_maps = []
    for c in range(NCORES):
        in_maps.append({
            "x2": np.ascontiguousarray(x2[c * BPC:(c + 1) * BPC]),
            "r_sigma": np.ascontiguousarray(r_sigma.reshape(1, 1)),
            "gate_weight": gate_weight,
        })
    res = run_bass_kernel_spmd(nc, in_maps, core_ids=list(range(NCORES)))
    x3 = np.concatenate([res.results[c]["x3"] for c in range(NCORES)], axis=0)
    logits = np.concatenate([res.results[c]["logitsT"].T for c in range(NCORES)], axis=0)
    return x3, logits



NE = 8
NTOK = NCORES * BPC * S
KTH = 4 * 1024 * 6


def build_l1ffn(nc, shared=None):
    ctx = nc.ctx
    x_in = nc.dram_tensor("x3", [BPC, S, D], F32, kind="ExternalInput").ap()
    lg_all = nc.dram_tensor("lg_all", [128, NTOK * NE // 128], F32, kind="ExternalInput").ap()
    lg_loc = nc.dram_tensor("lg_loc", [NE, BPC * S], F32, kind="ExternalInput").ap()
    w1 = nc.dram_tensor("w1", [NE, DH, D], F32, kind="ExternalInput").ap()
    b1 = nc.dram_tensor("b1", [NE, DH], F32, kind="ExternalInput")
    w2 = nc.dram_tensor("w2", [NE, D, DH], F32, kind="ExternalInput").ap()
    b2 = nc.dram_tensor("b2", [NE, D], F32, kind="ExternalInput").ap()
    x_out = nc.dram_tensor("xo", [BPC, S, D], F32, kind="ExternalOutput").ap()
    w1T = nc.dram_tensor("w1T_scr", [NE, D, DH], BF16, kind="Internal")
    w2T = nc.dram_tensor("w2T_scr", [NE, DH, D], BF16, kind="Internal")
    srow = nc.dram_tensor("srow2_scr", [1, 128], F32, kind="Internal")
    gts_d = nc.dram_tensor("gts_scr", [NE, BPC * S], F32, kind="Internal")

    top = ExitStack()
    sb = lambda name, shape, dt: top.enter_context(nc.sbuf_tensor(_uname(name), shape, dt))
    if shared is None:
        ident = sb("ident", [128, 128], BF16)
        identf = sb("identf", [128, 128], F32)
    else:
        ident, identf = shared["ident"], shared["identf"]
    ones_row_f = sb("ones_row_f", [1, 128], F32)
    ones_col_f = sb("ones_col_f", [128, 1], F32)
    LG = sb("LG", [128, NTOK * NE // 128], F32)
    cmpb = sb("cmpb", [128, NTOK * NE // 128], F32)
    bs = sb("bs", [128, 8], F32)
    br = sb("br", [1, 128], F32)
    gts = sb("gts", [NE, BPC * S], F32)
    b2f = sb("b2f", [NE, D], F32)
    b1all = sb("b1all", [128, NE * DH // 128], F32)
    gmask = sb("gmask", [NE, BPC * S], F32)

    p = Prog(nc, ctx)
    sD = p.sem("sD")
    sPE = p.sem("sPE")
    sV = p.chain["vector"]
    sA = p.chain["scalar"]
    sG = p.chain["gpsimd"]
    sO = p.sem("sO")
    sL = p.sem("sL")

    p.emit("gpsimd", lambda e: e.memset(ident[:], 1.0))
    p.emit("gpsimd", lambda e: e.affine_select(out=ident[:], in_=ident[:], pattern=[[1, 128]],
                                               compare_op=ALU.is_equal, fill=0.0, base=0,
                                               channel_multiplier=-1))
    p.emit("gpsimd", lambda e: e.memset(identf[:], 1.0))
    p.emit("gpsimd", lambda e: e.affine_select(out=identf[:], in_=identf[:], pattern=[[1, 128]],
                                               compare_op=ALU.is_equal, fill=0.0, base=0,
                                               channel_multiplier=-1))
    p.emit("gpsimd", lambda e: e.memset(ones_row_f[:], 1.0))
    p.emit("gpsimd", lambda e: e.memset(ones_col_f[:], 1.0), inc=sG)
    consts_g = sG.n

    ps_s = top.enter_context(nc.psum_tensor(_uname("ps_s"), [128, 8], F32))

    def _bisect():
        W = NTOK * NE // 128
        p.dma("sync", LG[:, :], lg_all[:, :], inc=sL)
        p.dma("sync", gts[0:NE, :], lg_loc[:, :], inc=sL)
        p.wait("vector", sL)
        p.emit("vector", lambda e: e.tensor_reduce(out=bs[:, 0:1], in_=LG[:, :], axis=AX.X,
                                                   op=ALU.min), inc=sV)
        p.emit("vector", lambda e: e.tensor_reduce(out=bs[:, 1:2], in_=LG[:, :], axis=AX.X,
                                                   op=ALU.max), inc=sV)
        p.wait("sync", sV)
        p.dma("sync", srow.ap(), bs[:, 0:1], inc=sL)
        p.wait("sync", sL)
        p.dma("sync", br[0:1, :], srow.ap(), inc=sL)
        p.wait("vector", sL)
        p.emit("vector", lambda e: e.tensor_reduce(out=bs[0:1, 4:5], in_=br[0:1, :], axis=AX.X,
                                                   op=ALU.min), inc=sV)
        p.wait("sync", sV)
        p.dma("sync", srow.ap(), bs[:, 1:2], inc=sL)
        p.wait("sync", sL)
        p.dma("sync", br[0:1, :], srow.ap(), inc=sL)
        p.wait("vector", sL)
        p.emit("vector", lambda e: e.tensor_reduce(out=bs[0:1, 5:6], in_=br[0:1, :], axis=AX.X,
                                                   op=ALU.max), inc=sV)
        p.wait("scalar", sV)
        p.emit("scalar", lambda e: e.activation(out=bs[0:1, 6:7], in_=bs[0:1, 4:5],
                                                func=AF.Abs, scale=1.0), inc=sA)
        p.emit("scalar", lambda e: e.activation(out=bs[0:1, 7:8], in_=bs[0:1, 5:6],
                                                func=AF.Abs, scale=1.0), inc=sA)
        p.wait("vector", sA)
        p.emit("vector", lambda e: e.scalar_tensor_tensor(out=bs[0:1, 4:5], in0=bs[0:1, 6:7],
                                                          scalar=-1e-3, in1=bs[0:1, 4:5],
                                                          op0=ALU.mult, op1=ALU.add), inc=sV)
        p.emit("vector", lambda e: e.tensor_scalar(out=bs[0:1, 4:5], in0=bs[0:1, 4:5],
                                                   scalar1=-1e-20, scalar2=None, op0=ALU.add),
               inc=sV)
        p.emit("vector", lambda e: e.scalar_tensor_tensor(out=bs[0:1, 5:6], in0=bs[0:1, 7:8],
                                                          scalar=1e-3, in1=bs[0:1, 5:6],
                                                          op0=ALU.mult, op1=ALU.add), inc=sV)
        p.emit("vector", lambda e: e.tensor_scalar(out=bs[0:1, 5:6], in0=bs[0:1, 5:6],
                                                   scalar1=1e-20, scalar2=None, op0=ALU.add),
               inc=sV)
        p.wait("tensor", sV)
        p.wait("tensor", sG, consts_g)
        p.emit("tensor", lambda e: e.matmul(ps_s[:, 0:2], ones_row_f[0:1, 0:128],
                                            bs[0:1, 4:6], start=True, stop=True), inc=sPE)
        p.wait("vector", sPE)
        p.emit("vector", lambda e: e.tensor_copy(out=bs[:, 0:2], in_=ps_s[:, 0:2]), inc=sV)
        for r in range(64):
            p.emit("vector", lambda e: e.tensor_tensor(out=bs[:, 2:3], in0=bs[:, 0:1],
                                                       in1=bs[:, 1:2], op=ALU.add), inc=sV)
            p.emit("vector", lambda e: e.tensor_scalar(out=bs[:, 2:3], in0=bs[:, 2:3],
                                                       scalar1=0.5, scalar2=None,
                                                       op0=ALU.mult), inc=sV)
            p.emit("vector", lambda e: e.tensor_scalar(out=cmpb[:, :], in0=LG[:, :],
                                                       scalar1=bs[:, 2:3], scalar2=None,
                                                       op0=ALU.is_lt), inc=sV)
            p.emit("vector", lambda e: e.tensor_reduce(out=bs[:, 4:5], in_=cmpb[:, :],
                                                       axis=AX.X, op=ALU.add), inc=sV)
            p.wait("tensor", sV)
            p.emit("tensor", lambda e: e.matmul(ps_s[0:1, 2:3], ones_col_f[:, 0:1],
                                                bs[:, 4:5], start=True, stop=True), inc=sPE)
            p.wait("vector", sPE)
            p.emit("vector", lambda e: e.tensor_copy(out=br[0:1, 0:1], in_=ps_s[0:1, 2:3]),
                   inc=sV)
            p.wait("tensor", sV)
            p.emit("tensor", lambda e: e.matmul(ps_s[:, 3:4], ones_row_f[0:1, 0:128],
                                                br[0:1, 0:1], start=True, stop=True), inc=sPE)
            p.wait("vector", sPE)
            p.emit("vector", lambda e: e.tensor_scalar(out=bs[:, 3:4], in0=ps_s[:, 3:4],
                                                       scalar1=float(KTH) - 0.5, scalar2=None,
                                                       op0=ALU.is_lt), inc=sV)
            p.emit("vector", lambda e: e.tensor_tensor(out=bs[:, 4:5], in0=bs[:, 2:3],
                                                       in1=bs[:, 0:1], op=ALU.subtract), inc=sV)
            p.emit("vector", lambda e: e.tensor_tensor(out=bs[:, 4:5], in0=bs[:, 4:5],
                                                       in1=bs[:, 3:4], op=ALU.mult), inc=sV)
            p.emit("vector", lambda e: e.tensor_tensor(out=bs[:, 0:1], in0=bs[:, 0:1],
                                                       in1=bs[:, 4:5], op=ALU.add), inc=sV)
            p.emit("vector", lambda e: e.tensor_scalar(out=bs[:, 5:6], in0=bs[:, 3:4],
                                                       scalar1=-1.0, scalar2=1.0,
                                                       op0=ALU.mult, op1=ALU.add), inc=sV)
            p.emit("vector", lambda e: e.tensor_tensor(out=bs[:, 4:5], in0=bs[:, 2:3],
                                                       in1=bs[:, 1:2], op=ALU.subtract), inc=sV)
            p.emit("vector", lambda e: e.tensor_tensor(out=bs[:, 4:5], in0=bs[:, 4:5],
                                                       in1=bs[:, 5:6], op=ALU.mult), inc=sV)
            p.emit("vector", lambda e: e.tensor_tensor(out=bs[:, 1:2], in0=bs[:, 1:2],
                                                       in1=bs[:, 4:5], op=ALU.add), inc=sV)
        p.emit("vector", lambda e: e.tensor_scalar(out=gmask[0:NE, :], in0=gts[0:NE, :],
                                                   scalar1=bs[0:NE, 0:1], scalar2=None,
                                                   op0=ALU.is_gt), inc=sV)
        p.emit("vector", lambda e: e.tensor_tensor(out=gts[0:NE, :], in0=gts[0:NE, :],
                                                   in1=gmask[0:NE, :], op=ALU.mult),
               inc=sV)
        p.wait("sync", sV)
        p.dma("sync", gts_d.ap(), gts[0:NE, :], inc=sL)
        gts_done[0] = sL.n

    gts_done = [0]
    _bisect()

    def _prep(ph):
        psb = lambda name, shape, dt: ph.enter_context(nc.sbuf_tensor(_uname(name), shape, dt))
        pps = lambda name, shape, dt: ph.enter_context(nc.psum_tensor(_uname(name), shape, dt))
        wf = psb("wf", [128, DH], F32)
        wb = psb("wb", [128, DH], BF16)
        tb = psb("tb", [128, 1024], BF16)
        ps_t = pps("ps_t", [128, 1024], BF16)
        sWl = p.sem("sWl")
        sTo = p.sem("sTo")
        p.wait("tensor", sG, consts_g)
        for e_ in range(NE):
            for nm, src, scr, o_, i_ in (("w1", w1, w1T, DH, D), ("w2", w2, w2T, D, DH)):
                for ot in range(o_ // 128):
                    p.wait("sync", sA)
                    p.dma("sync", wf[:, 0:i_],
                          src[e_, ot * 128:(ot + 1) * 128, :], inc=sWl)
                    p.wait("scalar", sWl)
                    p.emit("scalar", lambda e, i_=i_: e.activation(
                        out=wb[:, 0:i_], in_=wf[:, 0:i_], func=AF.Copy), inc=sA)
                    p.wait("tensor", sA)
                    for jb in range(i_ // 1024):
                        p.wait("tensor", sA)
                        for j8 in range(8):
                            p.emit("tensor", lambda e, jb=jb, j8=j8: e.transpose(
                                ps_t[:, j8 * 128:(j8 + 1) * 128],
                                wb[:, (jb * 8 + j8) * 128:(jb * 8 + j8 + 1) * 128],
                                ident[:]), inc=sPE if j8 == 7 else None)
                        p.wait("scalar", sPE)
                        p.wait("scalar", sTo)
                        p.emit("scalar", lambda e: e.activation(out=tb[:, :], in_=ps_t[:, :],
                                                                func=AF.Copy), inc=sA)
                        p.wait("sync", sA)
                        p.dma("sync",
                              bass.AP(scr, e_ * i_ * o_ + jb * 1024 * o_ + ot * 128,
                                      [[o_, 128], [128 * o_, 8], [1, 128]]),
                              tb[:, :].rearrange("p (a b) -> p a b", b=128), inc=sTo)
        p.wait("sync", sTo)
        p.wait("sync", sWl)
        p.barrier("prep1f")

    with ExitStack() as ph:
        _prep(ph)

    p.dma("sync", b2f[0:NE, :], b2[:, :], inc=sD)
    b2_d = sD.n
    sB1g = p.sem("sB1g")
    for e_ in range(NE):
        for ot in range(DH // 128):
            p.dma("sync", b1all[:, e_ * 32 + ot:e_ * 32 + ot + 1],
                  bass.AP(b1, e_ * DH + ot * 128, [[1, 128], [1, 1]]), inc=sB1g)
    b1_d = sB1g

    NC2 = 512

    def _chunk(c, ph):
        psb = lambda name, shape, dt: ph.enter_context(nc.sbuf_tensor(_uname(name), shape, dt))
        pps = lambda name, shape, dt: ph.enter_context(nc.psum_tensor(_uname(name), shape, dt))
        xt = psb("xt", [128, D], F32)
        sqs = psb("sqs", [128, D], F32)
        st = psb("st", [128, 8], F32)
        lnb = psb("lnb", [128, D], BF16)
        ln4T = [psb(f"ln4T{i}", [128, NC2], BF16) for i in range(DT)]
        he = psb("he", [128, 32 * NC2], BF16)
        accd = [psb(f"accd{i}", [128, NC2], F32) for i in range(DT)]
        gb = psb("gb", [128, NC2], F32)
        grow = psb("grow", [1, NC2], F32)
        wbuf = [psb(f"wbuf{i}", [128, DH], BF16) for i in range(2)]
        mixb = psb("mixb", [128, NC2], F32)
        foT = psb("foT", [128, D], F32)
        lno = psb("lno", [128, D], F32)
        ps_m = pps("ps_m", [128, NC2], F32)
        ps_m2 = pps("ps_m2", [128, NC2], F32)
        ps_tb = pps("ps_tb", [128, 1024], BF16)
        ps_tf = pps("ps_tf", [128, 1024], F32)
        ps_g = pps("ps_g", [128, NC2], F32)
        sW = [p.sem("sW0f"), p.sem("sW1f")]
        sX = p.sem("sXf")

        def he_t(t):
            return he[:, t * NC2:(t + 1) * NC2]

        p.emit("vector", lambda e: e.memset(st[:, 7:8], EPS), inc=sV)
        for it in range(NC2 // 128):
            tok = c * NC2 + it * 128
            b, s0 = tok // S, tok % S
            p.wait("sync", sV)
            p.wait("sync", sA)
            p.dma("sync", xt[:, :], x_in[b, s0:s0 + 128, :], inc=sX)
            p.wait("vector", sX)
            p.wait("scalar", sX)
            ln_emit(p, xt[:, :], lnb[:, :], sqs, st, sA, sV)
            p.wait("tensor", sV)
            p.wait("tensor", sA)
            for j in range(DT):
                p.emit("tensor", lambda e, j=j: e.transpose(
                    ps_tb[:, j * 128:(j + 1) * 128], lnb[:, j * 128:(j + 1) * 128],
                    ident[:]), inc=sPE if j == DT - 1 else None)
            p.wait("scalar", sPE)
            for j in range(DT):
                p.emit("scalar", lambda e, j=j, it=it: e.activation(
                    out=ln4T[j][:, it * 128:(it + 1) * 128],
                    in_=ps_tb[:, j * 128:(j + 1) * 128], func=AF.Copy),
                    inc=sA if j == DT - 1 else None)
            p.wait("tensor", sA)

        sGr = p.sem("sGr")
        p.wait("sync", sL, gts_done[0])
        for e_ in range(NE):
            p.wait("sync", sPE)
            p.dma("sync", grow[0:1, :],
                  bass.AP(gts_d, e_ * BPC * S + c * NC2, [[NC2, 1], [1, NC2]]), inc=sGr)
            p.wait("tensor", sGr)
            p.wait("tensor", sV)
            p.wait("tensor", sA)
            p.emit("tensor", lambda e: e.matmul(
                ps_g[:, :], ones_row_f[0:1, 0:128], grow[0:1, :],
                start=True, stop=True), inc=sPE)
            p.wait("vector", sPE)
            p.emit("vector", lambda e: e.tensor_copy(out=gb[:, :], in_=ps_g[:, :]), inc=sV)
            entry_pe = sPE.n
            mm_counts = {0: entry_pe, 1: entry_pe}
            evac_counts = {}
            entry_a, entry_v = sA.n, sV.n
            p.wait("scalar", b1_d)
            for ot in range(32):
                bi = ot % 2
                p.wait("sync", sPE, mm_counts[bi])
                p.dma("sync", wbuf[bi][:, 0:DT * 128].rearrange("p (a b) -> p a b", b=128),
                      bass.AP(w1T, e_ * D * DH + ot * 128, [[DH, 128], [128 * DH, DT], [1, 128]]),
                      inc=sW[bi])
                p.wait("tensor", sW[bi])
                ps_u = ps_m if bi == 0 else ps_m2
                if ot >= 2:
                    p.wait("tensor", sA, evac_counts[bi])
                else:
                    p.wait("tensor", sA, entry_a)
                    p.wait("tensor", sV, entry_v)
                for it in range(DT):
                    p.emit("tensor", lambda e, bi=bi, it=it, ps_u=ps_u: e.matmul(
                        ps_u[:, :], wbuf[bi][:, it * 128:(it + 1) * 128],
                        ln4T[it][:, :], start=(it == 0), stop=(it == DT - 1)),
                        inc=sPE if it == DT - 1 else None)
                mm_counts[bi] = sPE.n
                p.wait("scalar", sPE)
                p.emit("scalar", lambda e, ot=ot, ps_u=ps_u, e_=e_: e.activation(
                    out=he_t(ot), in_=ps_u[:, :], func=AF.Gelu_apprx_tanh,
                    bias=b1all[:, e_ * 32 + ot:e_ * 32 + ot + 1], scale=1.0), inc=sA)
                evac_counts[bi] = sA.n
            entry_pe = sPE.n
            mm_counts = {0: entry_pe, 1: entry_pe}
            evac_counts = {}
            entry_a, entry_v = sA.n, sV.n
            for ot in range(DT):
                bi = ot % 2
                p.wait("sync", sPE, mm_counts[bi])
                p.dma("sync", wbuf[bi][:, :].rearrange("p (a b) -> p a b", b=128),
                      bass.AP(w2T, e_ * DH * D + ot * 128, [[D, 128], [128 * D, 32], [1, 128]]),
                      inc=sW[bi])
                p.wait("tensor", sW[bi])
                ps_u = ps_m if bi == 0 else ps_m2
                if ot >= 2:
                    p.wait("tensor", sV, evac_counts[bi])
                else:
                    p.wait("tensor", sA, entry_a)
                    p.wait("tensor", sV, entry_v)
                for it in range(32):
                    p.emit("tensor", lambda e, bi=bi, it=it, ps_u=ps_u: e.matmul(
                        ps_u[:, :], wbuf[bi][:, it * 128:(it + 1) * 128],
                        he_t(it), start=(it == 0), stop=(it == 31)),
                        inc=sPE if it == 31 else None)
                mm_counts[bi] = sPE.n
                p.wait("vector", sPE)
                if e_ == 0:
                    p.emit("vector", lambda e, ot=ot, ps_u=ps_u: e.tensor_tensor(
                        out=accd[ot][:, :], in0=ps_u[:, :], in1=gb[:, :], op=ALU.mult),
                        inc=sV)
                else:
                    p.emit("vector", lambda e, ot=ot, ps_u=ps_u: e.tensor_tensor(
                        out=mixb[:, :], in0=ps_u[:, :], in1=gb[:, :], op=ALU.mult), inc=sV)
                    p.emit("vector", lambda e, ot=ot: e.tensor_tensor(
                        out=accd[ot][:, :], in0=accd[ot][:, :], in1=mixb[:, :],
                        op=ALU.add), inc=sV)
                evac_counts[bi] = sV.n

        p.wait("tensor", sD, b2_d)
        p.wait("tensor", sV)
        for ot in range(DT):
            p.emit("tensor", lambda e, ot=ot: e.matmul(
                ps_g[:, :], b2f[0:NE, ot * 128:(ot + 1) * 128],
                gts[0:NE, c * NC2:(c + 1) * NC2], start=True, stop=True), inc=sPE)
            p.wait("vector", sPE)
            p.emit("vector", lambda e, ot=ot: e.tensor_tensor(
                out=accd[ot][:, :], in0=accd[ot][:, :], in1=ps_g[:, :], op=ALU.add), inc=sV)
            p.wait("tensor", sV)

        for it in range(NC2 // 128):
            tok = c * NC2 + it * 128
            b, s0 = tok // S, tok % S
            p.wait("tensor", sV)
            p.wait("tensor", sA)
            for j in range(DT):
                p.emit("tensor", lambda e, j=j, it=it: e.transpose(
                    ps_tf[:, j * 128:(j + 1) * 128],
                    accd[j][:, it * 128:(it + 1) * 128], identf[:]),
                    inc=sPE if j == DT - 1 else None)
            p.wait("vector", sPE)
            p.emit("vector", lambda e: e.tensor_copy(out=foT[:, :], in_=ps_tf[:, 0:D]), inc=sV)
            p.wait("vector", sV)
            p.wait("scalar", sV)
            ln_emit(p, foT[:, :], lno[:, :], sqs, st, sA, sV)
            p.wait("sync", sV)
            p.wait("sync", sA)
            p.dma("sync", xt[:, :], x_in[b, s0:s0 + 128, :], inc=sX)
            p.wait("vector", sX)
            p.emit("vector", lambda e: e.tensor_tensor(out=lno[:, :], in0=lno[:, :],
                                                       in1=xt[:, :], op=ALU.add), inc=sV)
            p.wait("sync", sV)
            p.dma("sync", x_out[b, s0:s0 + 128, :], lno[:, :], inc=sO)
            p.wait("vector", sO)
        p.wait("sync", sO)
        p.barrier(f"f{c}")

    for c in range(BPC * S // NC2):
        with ExitStack() as ph:
            _chunk(c, ph)

    p.wait("sync", sO)
    p.realize()
    top.close()
    return nc


def _l1ffn_launch(x3, logits_all, gate_weight, w1, b1, w2, b2):
    nc = bass.Bass("TRN2", target_bir_lowering=False)
    build_l1ffn(nc)
    lg128 = np.ascontiguousarray(logits_all.reshape(128, -1))
    in_maps = []
    for c in range(NCORES):
        in_maps.append({
            "x3": np.ascontiguousarray(x3[c * BPC:(c + 1) * BPC]),
            "lg_all": lg128,
            "lg_loc": np.ascontiguousarray(
                logits_all[c * BPC * S:(c + 1) * BPC * S, :].T),
            "w1": w1, "b1": b1, "w2": w2, "b2": b2,
        })
    res = run_bass_kernel_spmd(nc, in_maps, core_ids=list(range(NCORES)))
    return np.concatenate([res.results[c]["xo"] for c in range(NCORES)], axis=0)



def build_prefix(nc):
    ctx = nc.ctx
    x1i = nc.dram_tensor("x1i_scr", [BPC, S, D], F32, kind="Internal").ap()
    x2i = nc.dram_tensor("x2i_scr", [BPC, S, D], F32, kind="Internal").ap()
    p = Prog(nc, ctx)
    sh = {}
    sb = lambda name, shape, dt: ctx.enter_context(nc.sbuf_tensor(_uname(name), shape, dt))
    sh["ident"] = sb("ident", [128, 128], BF16)
    sh["identf"] = sb("identf", [128, 128], F32)
    sh["idxbase"] = sb("idxbase", [128, 1], I32)
    sh["ones_col"] = sb("ones_col", [128, 1], BF16)
    sh["ones_row"] = sb("ones_row", [1, 128], BF16)
    sh["ones_row_f"] = sb("ones_row_f", [1, 128], F32)
    p.emit("gpsimd", lambda e: e.memset(sh["ident"][:], 1.0))
    p.emit("gpsimd", lambda e: e.affine_select(out=sh["ident"][:], in_=sh["ident"][:],
                                               pattern=[[1, 128]], compare_op=ALU.is_equal,
                                               fill=0.0, base=0, channel_multiplier=-1))
    p.emit("gpsimd", lambda e: e.memset(sh["identf"][:], 1.0))
    p.emit("gpsimd", lambda e: e.affine_select(out=sh["identf"][:], in_=sh["identf"][:],
                                               pattern=[[1, 128]], compare_op=ALU.is_equal,
                                               fill=0.0, base=0, channel_multiplier=-1))
    p.emit("gpsimd", lambda e: e.iota(sh["idxbase"][:], [[0, 1]], base=0,
                                      channel_multiplier=-1))
    p.emit("gpsimd", lambda e: e.memset(sh["ones_col"][:], 1.0))
    p.emit("gpsimd", lambda e: e.memset(sh["ones_row"][:], 1.0))
    p.emit("gpsimd", lambda e: e.memset(sh["ones_row_f"][:], 1.0))
    build_l0attn(nc, p=p, xout_ap=x1i, shared=sh)
    p.barrier("sec01")
    build_l0ffn(nc, p=p, xin_ap=x1i, xout_ap=x2i, shared=sh)
    p.barrier("sec12")
    build_l1attn(nc, p=p, xin_ap=x2i, shared=sh)
    p.realize()
    for t in p.deferred:
        t.close()
    return nc


def _prefix_launch(x, w_in, w_gate, b_gate, amp, sharp, offs,
                   w1_w, w1_s, w2_w, w2_s, h1_w, h1_s, h2_w, h2_s, w3_w, w3_s,
                   r_sigma, gate_weight):
    nc = bass.Bass("TRN2", target_bir_lowering=False)
    build_prefix(nc)
    in_maps = []
    for c in range(NCORES):
        in_maps.append({
            "x": np.ascontiguousarray(x[c * BPC:(c + 1) * BPC]),
            "w_in": w_in, "w_gate": w_gate,
            "b_gate": np.ascontiguousarray(b_gate.reshape(1, -1)),
            "amp": amp, "sharp": sharp, "offs": offs,
            "w1_w": w1_w, "w1_s": np.ascontiguousarray(w1_s.reshape(1, -1)),
            "w2_w": w2_w, "w2_s": np.ascontiguousarray(w2_s.reshape(1, -1)),
            "h1_w": h1_w, "h1_s": np.ascontiguousarray(h1_s.reshape(1, -1)),
            "h2_w": h2_w, "h2_s": np.ascontiguousarray(h2_s.reshape(1, -1)),
            "w3_w": w3_w, "w3_s": np.ascontiguousarray(w3_s.reshape(1, -1)),
            "r_sigma": np.ascontiguousarray(r_sigma.reshape(1, 1)),
            "gate_weight": gate_weight,
        })
    res = run_bass_kernel_spmd(nc, in_maps, core_ids=list(range(NCORES)))
    x3 = np.concatenate([res.results[c]["x3"] for c in range(NCORES)], axis=0)
    logits = np.concatenate([res.results[c]["logitsT"].T for c in range(NCORES)], axis=0)
    return x3, logits



def _l0attn_launch(x, w_in, w_gate, b_gate, amp, sharp, offs):
    nc = bass.Bass("TRN2", target_bir_lowering=False)
    build_l0attn(nc)
    in_maps = []
    for c in range(NCORES):
        in_maps.append({
            "x": np.ascontiguousarray(x[c * BPC:(c + 1) * BPC]),
            "w_in": w_in, "w_gate": w_gate,
            "b_gate": np.ascontiguousarray(b_gate.reshape(1, -1)),
            "amp": amp, "sharp": sharp, "offs": offs,
        })
    res = run_bass_kernel_spmd(nc, in_maps, core_ids=list(range(NCORES)))
    return np.concatenate([res.results[c]["x1"] for c in range(NCORES)], axis=0)


def kernel(L0_attn_x, L0_attn_w_in, L0_attn_w_gate, L0_attn_b_gate,
           L0_attn_amplitudes, L0_attn_sharpness, L0_attn_offsets,
           L0_ffn_w1_w, L0_ffn_w1_s, L0_ffn_w2_w, L0_ffn_w2_s,
           L0_ffn_h1_w, L0_ffn_h1_s, L0_ffn_h2_w, L0_ffn_h2_s,
           L0_ffn_w3_w, L0_ffn_w3_s,
           L1_attn_r_sigma,
           L1_ffn_gate_weight, L1_ffn_w1, L1_ffn_b1, L1_ffn_w2, L1_ffn_b2):
    f = np.asarray
    x3, logits = _prefix_launch(
        f(L0_attn_x), f(L0_attn_w_in), f(L0_attn_w_gate), f(L0_attn_b_gate),
        f(L0_attn_amplitudes), f(L0_attn_sharpness), f(L0_attn_offsets),
        f(L0_ffn_w1_w), f(L0_ffn_w1_s), f(L0_ffn_w2_w), f(L0_ffn_w2_s),
        f(L0_ffn_h1_w), f(L0_ffn_h1_s), f(L0_ffn_h2_w), f(L0_ffn_h2_s),
        f(L0_ffn_w3_w), f(L0_ffn_w3_s),
        f(L1_attn_r_sigma), f(L1_ffn_gate_weight))
    out = _l1ffn_launch(x3, logits, f(L1_ffn_gate_weight), f(L1_ffn_w1), f(L1_ffn_b1),
                        f(L1_ffn_w2), f(L1_ffn_b2))
    return out
